# version 02
# baseline (speedup 1.0000x reference)
import contextlib
import numpy as np
import ml_dtypes
import concourse.bass as bass
import concourse.mybir as mybir
from concourse.bass_utils import run_bass_kernel_spmd
from concourse.alu_op_type import AluOpType as OP

F32 = mybir.dt.float32
F16 = mybir.dt.float16
BF16 = mybir.dt.bfloat16
FP8 = mybir.dt.float8e4
AF = mybir.ActivationFunctionType
AX = mybir.AxisListType

D = 1024
T = 2048
DC = D // 128
TT = T // 128
NSEQ = 2
QB = 256
NQB = T // QB
SCALE = float(D) ** (-0.25)
EPS = 1e-5
LAM0 = 0.8
R1024 = 1.0 / 1024.0
LN02 = float(np.log(0.2))


class Sem:
    def __init__(self, E, nc, name):
        self.h = E(nc.semaphore(name))
        self.n = 0

    def inc(self, res, k=1):
        res.then_inc(self.h, k)
        self.n += k


def build_program(upto=None):
    nc = bass.Bass("TRN2", target_bir_lowering=False)

    x_d = nc.dram_tensor("x", [NSEQ * T, D], F32, kind="ExternalInput").ap()
    wq12_d = nc.dram_tensor("wq12", [D, 2 * D], F32, kind="ExternalInput").ap()
    wk12_d = nc.dram_tensor("wk12", [D, 2 * D], F32, kind="ExternalInput").ap()
    wv_d = nc.dram_tensor("wv", [D, D], F32, kind="ExternalInput").ap()
    fw_d = nc.dram_tensor("fw", [D, D], F32, kind="ExternalInput").ap()
    fb_d = nc.dram_tensor("fb", [D], F32, kind="ExternalInput").ap()
    lq1_d = nc.dram_tensor("lq1", [D], F32, kind="ExternalInput").ap()
    lk1_d = nc.dram_tensor("lk1", [D], F32, kind="ExternalInput").ap()
    lq2_d = nc.dram_tensor("lq2", [D], F32, kind="ExternalInput").ap()
    lk2_d = nc.dram_tensor("lk2", [D], F32, kind="ExternalInput").ap()
    id32_d = nc.dram_tensor("id32", [128, 128], F32, kind="ExternalInput").ap()
    id8_d = nc.dram_tensor("id8", [128, 128], FP8, kind="ExternalInput").ap()
    out_d = nc.dram_tensor("out", [NSEQ * T, D], F32, kind="ExternalOutput").ap()
    m_hbm = [[nc.dram_tensor(f"m{mi}{p}", [D, D], BF16).ap() for p in ("h", "l")]
             for mi in (1, 2)]
    rsp_d = nc.dram_tensor("rsp", [T, D], F32).ap()
    rsp2_d = nc.dram_tensor("rsp2", [T, D], F32).ap()

    with contextlib.ExitStack() as ES:
        E = ES.enter_context
        AR_BYTES = 212224
        arena = E(nc.sbuf_tensor("arena", [128, AR_BYTES // 4], F32))

        def fview(off_b, words):
            return arena[:, off_b // 4: off_b // 4 + words]

        def bview(off_b, elems):
            return arena[:, off_b // 4: off_b // 4 + (elems + 1) // 2].bitcast(BF16)[:, :elems]

        def qview(off_b, elems):
            return arena[:, off_b // 4: off_b // 4 + (elems + 3) // 4].bitcast(FP8)[:, :elems]

        def hview(off_b, elems):
            return arena[:, off_b // 4: off_b // 4 + (elems + 1) // 2].bitcast(F16)[:, :elems]

        cur = [0]

        def alloc(nbytes, align=64):
            o = (cur[0] + align - 1) // align * align
            cur[0] = o + nbytes
            assert cur[0] <= AR_BYTES, f"arena overflow {cur[0]}"
            return o

        o_wqT = alloc(16384)
        o_hbias = alloc(4096)
        o_id32 = alloc(512)
        o_id8 = alloc(128)
        o_rows = alloc(512)
        o_st = alloc(2048)
        o_H = alloc(65536)
        o_RV = alloc(65536)
        o_tmp = alloc(16384)
        o_E1 = alloc(8192)
        o_Dm = alloc(8192)
        o_PT = alloc(8192)
        o_Mbuf = alloc(4096)
        o_Mbuf2 = alloc(4096)
        o_Rio = alloc(4096)
        o_Rio2 = alloc(4096)

        wqT = bview(o_wqT, 8192)
        wqT3 = wqT.rearrange("p (cc f) -> p cc f", f=1024)
        hbias = fview(o_hbias, 1024)
        id32 = fview(o_id32, 128)
        id8 = qview(o_id8, 128)

        rowsv = fview(o_rows, 128)

        def row(i):
            return rowsv[:, i:i + 1]

        r_nm1, r_nm2, r_z1, r_z2 = row(4), row(5), row(6), row(7)
        r_srow, r_epsZ, r_msq = row(8), row(9), row(10)
        r_t1, r_t2, r_eps, r_lamneg = row(12), row(13), row(14), row(15)
        r_trow, r_l, r_e1, r_e2 = row(16), row(17), row(18), row(19)
        r_zero, r_mhalf, r_ln02, r_rq = row(21), row(22), row(23), row(24)
        r_sum, r_mean, r_var, r_rstd, r_a, r_mm, r_u0 = (
            row(25), row(26), row(27), row(28), row(29), row(30), row(31))

        stB = fview(o_st, 512)
        stB3 = stB[:, 0:192].rearrange("p (i s) -> p i s", s=12)
        junk64 = fview(o_st + 768, 64)
        junk64_3 = junk64.rearrange("p (i s) -> p i s", s=4)
        sum16 = fview(o_st + 1024, 16)
        cv16 = fview(o_st + 1088, 16)
        m2s16 = fview(o_st + 1152, 16)
        mean16 = fview(o_st + 1216, 16)
        cvn16 = fview(o_st + 1280, 16)
        a16 = fview(o_st + 1344, 16)
        mm16 = fview(o_st + 1408, 16)
        u16 = fview(o_st + 1472, 16)
        u16b = fview(o_st + 1536, 16)
        l16 = fview(o_st + 1600, 16)
        rstd16 = fview(o_st + 1664, 16)
        eps16 = fview(o_st + 1728, 16)
        st4 = fview(o_st, 512)[:, 0:12]
        junk4 = fview(o_st + 768, 64)[:, 48:52]

        hfm_h = bview(o_H, 16384)
        hfm_l = bview(o_H + 32768, 16384)
        hfm3_h = hfm_h.rearrange("p (cc t) -> p cc t", t=2048)
        hfm3_l = hfm_l.rearrange("p (cc t) -> p cc t", t=2048)
        wqt_h = bview(o_H, 8192)
        wqt_l = bview(o_H + 16384, 8192)
        wkt_h = bview(o_H + 32768, 8192)
        wkt_l = bview(o_H + 49152, 8192)
        wqt3 = [(wqt_h.rearrange("p (ec a) -> p ec a", a=1024),
                 wqt_l.rearrange("p (ec a) -> p ec a", a=1024)),
                (wkt_h.rearrange("p (ec a) -> p ec a", a=1024),
                 wkt_l.rearrange("p (ec a) -> p ec a", a=1024))]

        R = fview(o_RV, 16384)
        R3 = R.rearrange("p (i c) -> p i c", c=1024)
        v_h = bview(o_RV, 16384)
        v_l = bview(o_RV + 32768, 16384)
        wstage = fview(o_RV, 1024)
        wstage2 = fview(o_RV + 4096, 1024)
        mst_h = bview(o_RV + 8192, 512)
        mst_l = bview(o_RV + 10240, 512)
        qa_bf = bview(o_RV + 12288, 512)
        qb_bf = bview(o_RV + 14336, 512)
        qmask = fview(o_RV + 16384, 512)
        q8st = qview(o_RV + 18432, 512)
        qsel32 = fview(o_RV + 20480, 512)
        lam_bc = [fview(o_RV + 24576 + i * 4096, 1024) for i in range(4)]

        tmp_h = [bview(o_tmp, 2048), bview(o_tmp + 8192, 2048)]
        tmp_l = [bview(o_tmp + 4096, 2048), bview(o_tmp + 12288, 2048)]
        E1 = fview(o_E1, 2048)
        Dm = fview(o_Dm, 2048)
        PT_h = bview(o_PT, 2048)
        PT_l = bview(o_PT + 4096, 2048)
        Mb_h = [bview(o_Mbuf, 1024), bview(o_Mbuf2, 1024)]
        Mb_l = [bview(o_Mbuf + 2048, 1024), bview(o_Mbuf2 + 2048, 1024)]
        Rio = fview(o_Rio, 1024)
        Rio2 = fview(o_Rio2, 1024)
        wv_h = bview(o_tmp, 8192)
        wv_l = bview(o_E1, 8192)
        h1 = fview(o_PT, 1024)
        q8t = qview(o_PT + 4096, 1024)
        q8T = bview(o_PT + 5120, 1024)
        ffn16 = hview(o_Mbuf, 1024)
        wvstage = fview(o_Rio, 1024)

        psA = E(nc.psum_tensor("psA", [128, 2048], F32))
        psB = E(nc.psum_tensor("psB", [128, 2048], F32))
        psB8 = psB.bitcast(FP8)

        sL = Sem(E, nc, "sL")
        sG = Sem(E, nc, "sG")
        sV = Sem(E, nc, "sV")
        sS = Sem(E, nc, "sS")
        sT = Sem(E, nc, "sT")
        sMb = Sem(E, nc, "sMb")
        ALL = (sL, sG, sV, sS, sT)

        def wait_all(eng, *sems):
            for s in (sems or ALL):
                if s.n:
                    eng.wait_ge(s.h, s.n)

        with nc.Block() as B:
            @B.sync
            def _(sync):
                sL.inc(sync.dma_start(out=id32[:], in_=id32_d[:]), 16)
                sL.inc(sync.dma_start(out=id8[:], in_=id8_d[:]), 16)

            @B.gpsimd
            def _(g):
                for i, dd in enumerate((lq1_d, lk1_d, lq2_d, lk2_d)):
                    sG.inc(g.dma_start(out=lam_bc[i][:],
                                       in_=dd.unsqueeze(0).partition_broadcast(128)), 16)
                sG.inc(g.dma_start(out=wstage2[:],
                                   in_=fb_d.unsqueeze(0).partition_broadcast(128)), 16)

            @B.vector
            def _(v):
                v.memset(r_eps[:], EPS)
                v.memset(r_zero[:], 0.0)
                v.memset(r_mhalf[:], -0.5)
                v.memset(r_ln02[:], LN02)
                v.memset(eps16[:], EPS)
                v.wait_ge(sG.h, sG.n)
                v.tensor_mul(lam_bc[0][:], lam_bc[0][:], lam_bc[1][:])
                v.tensor_mul(lam_bc[2][:], lam_bc[2][:], lam_bc[3][:])
                v.reduce_sum(r_t1[:], lam_bc[0][:], axis=AX.X)
                v.reduce_sum(r_t2[:], lam_bc[2][:], axis=AX.X)
                v.drain()
                sV.inc(v.tensor_scalar_mul(hbias[:], wstage2[:], 0.5))

            @B.scalar
            def _(s):
                s.wait_ge(sV.h, sV.n)
                s.activation(r_e1[:], r_t1[:], AF.Exp, bias=r_zero[:, 0:1], scale=1.0)
                sS.inc(s.activation(r_e2[:], r_t2[:], AF.Exp, bias=r_zero[:, 0:1], scale=1.0))

            @B.vector
            def _(v):
                v.wait_ge(sS.h, sS.n)
                v.tensor_sub(r_t1[:], r_e2[:], r_e1[:])
                v.drain()
                v.tensor_single_scalar(r_lamneg[:], r_t1[:], -LAM0, OP.add)
                sV.inc(v.drain())

        for fc in range(DC):
            with nc.Block() as B:
                @B.sync
                def _(sync):
                    sync.wait_ge(sV.h, sV.n)
                    sL.inc(sync.dma_start(out=wstage[:],
                                          in_=fw_d[fc * 128:(fc + 1) * 128, :]), 16)

                @B.vector
                def _(v):
                    v.wait_ge(sL.h, sL.n)
                    sV.inc(v.tensor_scalar(wstage2[:], wstage[:], -448.0, 448.0,
                                           OP.max, OP.min))

                @B.tensor
                def _(t):
                    t.wait_ge(sV.h, sV.n)
                    for cc in range(DC):
                        bank, j = divmod(cc, 4)
                        r = t.matmul(psB[:, bank * 512 + j * 128:bank * 512 + (j + 1) * 128],
                                     wstage2[:, cc * 128:(cc + 1) * 128], id32[:],
                                     is_transpose=True, start=(j == 0), stop=(j == 3))
                    sT.inc(r)

                @B.vector
                def _(v):
                    v.wait_ge(sT.h, sT.n)
                    for bank in range(2):
                        pb = psB[:, bank * 512:(bank + 1) * 512]
                        v.tensor_copy(q8st[:], pb)
                        v.tensor_copy(qa_bf[:], q8st[:])
                        v.tensor_single_scalar(q8st[:], pb, 0.5, OP.mult)
                        v.tensor_single_scalar(qb_bf[:], q8st[:], 2.0, OP.mult)
                        v.tensor_single_scalar(qmask[:], pb, 1.0, OP.is_lt)
                        v.tensor_single_scalar(qsel32[:], pb, -1.0, OP.is_gt)
                        v.drain()
                        v.tensor_mul(qmask[:], qmask[:], qsel32[:])
                        v.tensor_sub(qsel32[:], qa_bf[:], qb_bf[:])
                        v.drain()
                        v.tensor_mul(qsel32[:], qsel32[:], qmask[:])
                        v.drain()
                        dst = wqT3[:, bank * 4:bank * 4 + 4, fc * 128:(fc + 1) * 128]
                        v.tensor_tensor(dst, qb_bf[:].rearrange("p (s f) -> p s f", f=128),
                                        qsel32[:].rearrange("p (s f) -> p s f", f=128),
                                        OP.add)
                    sV.inc(v.drain())

        for mi in range(2):
            for wi, wd in enumerate((wq12_d, wk12_d)):
                th3, tl3 = wqt3[wi]
                for dc in range(DC):
                    with nc.Block() as B:
                        @B.sync
                        def _(sync):
                            sync.wait_ge(sV.h, sV.n)
                            sync.wait_ge(sT.h, sT.n)
                            sL.inc(sync.dma_start(
                                out=wstage[:],
                                in_=wd[dc * 128:(dc + 1) * 128, mi * D:(mi + 1) * D]), 16)

                        @B.tensor
                        def _(t):
                            t.wait_ge(sL.h, sL.n)
                            t.wait_ge(sV.h, sV.n)
                            for ec in range(DC):
                                bank, j = divmod(ec, 4)
                                r = t.matmul(
                                    psB[:, bank * 512 + j * 128:bank * 512 + (j + 1) * 128],
                                    wstage[:, ec * 128:(ec + 1) * 128], id32[:],
                                    is_transpose=True, start=(j == 0), stop=(j == 3))
                            sT.inc(r)

                        @B.vector
                        def _(v):
                            v.wait_ge(sT.h, sT.n)
                            for bank in range(2):
                                pb3 = psB[:, bank * 512:(bank + 1) * 512].rearrange(
                                    "p (s a) -> p s a", a=128)
                                dh = th3[:, bank * 4:bank * 4 + 4, dc * 128:(dc + 1) * 128]
                                dl = tl3[:, bank * 4:bank * 4 + 4, dc * 128:(dc + 1) * 128]
                                v.tensor_copy(dh, pb3)
                                v.tensor_tensor(dl, pb3, dh, OP.subtract)
                            sV.inc(v.nop())
            for ac in range(DC):
                for bh in range(2):
                    with nc.Block() as B:
                        @B.tensor
                        def _(t):
                            t.wait_ge(sV.h, sV.n)
                            t.wait_ge(sG.h, sG.n)
                            pb = psA[:, 0:512]
                            for ec in range(DC):
                                lh = wqt_h[:, ec * 1024 + ac * 128:ec * 1024 + (ac + 1) * 128]
                                ll = wqt_l[:, ec * 1024 + ac * 128:ec * 1024 + (ac + 1) * 128]
                                rh = wkt_h[:, ec * 1024 + bh * 512:ec * 1024 + (bh + 1) * 512]
                                rl = wkt_l[:, ec * 1024 + bh * 512:ec * 1024 + (bh + 1) * 512]
                                t.matmul(pb, lh, rh, start=(ec == 0), stop=False)
                                t.matmul(pb, lh, rl, start=False, stop=False)
                                r = t.matmul(pb, ll, rh, start=False, stop=(ec == DC - 1))
                            sT.inc(r)

                        @B.vector
                        def _(v):
                            v.wait_ge(sT.h, sT.n)
                            pb = psA[:, 0:512]
                            v.tensor_single_scalar(mst_h[:], pb, SCALE, OP.mult)
                            sV.inc(v.scalar_tensor_tensor(mst_l[:], pb, SCALE, mst_h[:],
                                                          OP.mult, OP.subtract))

                        @B.gpsimd
                        def _(g):
                            g.wait_ge(sV.h, sV.n)
                            dh = m_hbm[mi][0][ac * 128:(ac + 1) * 128, bh * 512:(bh + 1) * 512]
                            dl = m_hbm[mi][1][ac * 128:(ac + 1) * 128, bh * 512:(bh + 1) * 512]
                            sG.inc(g.dma_start(out=dh, in_=mst_h[:]), 16)
                            sG.inc(g.dma_start(out=dl, in_=mst_l[:]), 16)

        if upto == "prologue":
            with nc.Block() as B:
                @B.gpsimd
                def _(g):
                    wait_all(g)
                    sG.inc(g.dma_start(out=out_d[0:1024, :], in_=m_hbm[0][0][:]), 16)
                    sG.inc(g.dma_start(out=out_d[1024:2048, :], in_=m_hbm[0][1][:]), 16)
                    sG.inc(g.dma_start(out=out_d[2048:3072, :], in_=m_hbm[1][0][:]), 16)
                    sG.inc(g.dma_start(out=out_d[3072:4096, :], in_=m_hbm[1][1][:]), 16)
            _finish(nc, ALL)
            return nc

        def ln_stats16():
            with nc.Block() as B:
                @B.vector
                def _(v):
                    wait_all(v, sS, sT, sG, sL)
                    for i in range(TT):
                        v.bn_stats(stB3[:, i, 0:6], R[:, i * 1024:i * 1024 + 512])
                        v.bn_stats(stB3[:, i, 6:12], R[:, i * 1024 + 512:(i + 1) * 1024])
                    v.drain()
                    means3 = stB3[:, :, 1:12:3]
                    cvars3 = stB3[:, :, 2:12:3]
                    v.tensor_tensor(junk64_3, means3, means3, OP.mult)
                    v.reduce_sum(sum16[:], means3, axis=AX.X)
                    v.reduce_sum(cv16[:], cvars3, axis=AX.X)
                    v.drain()
                    v.reduce_sum(m2s16[:], junk64_3, axis=AX.X)
                    v.tensor_scalar_mul(mean16[:], sum16[:], 0.25)
                    v.tensor_scalar_mul(cvn16[:], cv16[:], R1024)
                    v.drain()
                    v.scalar_tensor_tensor(a16[:], m2s16[:], 0.25, cvn16[:],
                                           OP.mult, OP.add)
                    v.tensor_mul(mm16[:], mean16[:], mean16[:])
                    v.drain()
                    v.scalar_tensor_tensor(u16[:], mm16[:], -1.0, a16[:],
                                           OP.mult, OP.add)
                    v.drain()
                    v.tensor_add(u16b[:], u16[:], eps16[:])
                    sV.inc(v.drain())

                @B.scalar
                def _(s):
                    s.wait_ge(sV.h, sV.n)
                    s.activation(l16[:], u16b[:], AF.Ln, bias=r_zero[:, 0:1], scale=1.0)
                    sS.inc(s.activation(rstd16[:], l16[:], AF.Exp, bias=r_zero[:, 0:1],
                                        scale=r_mhalf[:, 0:1]))

        def ln_tile_single(src_ap):
            with nc.Block() as B:
                @B.vector
                def _(v):
                    wait_all(v, sS, sT, sG, sL)
                    v.bn_stats(st4[:, 0:6], src_ap[:, 0:512])
                    v.bn_stats(st4[:, 6:12], src_ap[:, 512:1024])
                    v.drain()
                    v.tensor_mul(junk4[:], st4[:, 1:12:3], st4[:, 1:12:3])
                    v.reduce_sum(r_sum[:], st4[:, 1:12:3], axis=AX.X)
                    v.reduce_sum(r_var[:], st4[:, 2:12:3], axis=AX.X)
                    v.drain()
                    v.reduce_sum(r_t1[:], junk4[:], axis=AX.X)
                    v.tensor_scalar_mul(r_mean[:], r_sum[:], 0.25)
                    v.tensor_scalar_mul(r_t2[:], r_var[:], R1024)
                    v.drain()
                    v.scalar_tensor_tensor(r_a[:], r_t1[:], 0.25, r_t2[:],
                                           OP.mult, OP.add)
                    v.tensor_mul(r_mm[:], r_mean[:], r_mean[:])
                    v.drain()
                    v.scalar_tensor_tensor(r_u0[:], r_mm[:], -1.0, r_a[:],
                                           OP.mult, OP.add)
                    sV.inc(v.drain())

                @B.scalar
                def _(s):
                    s.wait_ge(sV.h, sV.n)
                    s.activation(r_l[:], r_u0[:], AF.Ln, bias=r_eps[:, 0:1], scale=1.0)
                    sS.inc(s.activation(r_rstd[:], r_l[:], AF.Exp, bias=r_zero[:, 0:1],
                                        scale=r_mhalf[:, 0:1]))

        def ffn_core(src_ap, dst_ap, mean_ap, rstd_ap):
            with nc.Block() as B:
                @B.vector
                def _(v):
                    v.wait_ge(sS.h, sS.n)
                    wait_all(v, sT)
                    sV.inc(v.tensor_scalar(q8t[:], src_ap, mean_ap, rstd_ap,
                                           OP.subtract, OP.mult))

                @B.tensor
                def _(t):
                    t.wait_ge(sV.h, sV.n)
                    for cc in range(DC):
                        r = t.matmul(psB8[:, cc * 256:cc * 256 + 256:2],
                                     q8t[:, cc * 128:(cc + 1) * 128], id8[:],
                                     is_transpose=True, start=(cc == 0), stop=(cc == 7))
                    sT.inc(r)

                @B.vector
                def _(v):
                    v.wait_ge(sT.h, sT.n)
                    src3 = psB8[:, 0:2048].rearrange("p (s t) -> p s t", t=256)[:, :, ::2]
                    sV.inc(v.tensor_copy(q8T[:].rearrange("p (s t) -> p s t", t=128), src3))

                @B.tensor
                def _(t):
                    t.wait_ge(sV.h, sV.n)
                    for fh in range(2):
                        pb = psA[:, fh * 512:(fh + 1) * 512]
                        for cc in range(DC):
                            r = t.matmul(pb, q8T[:, cc * 128:(cc + 1) * 128],
                                         wqT[:, cc * 1024 + fh * 512:cc * 1024 + (fh + 1) * 512],
                                         start=(cc == 0), stop=(cc == DC - 1))
                    sT.inc(r)

                @B.scalar
                def _(s):
                    s.wait_ge(sT.h, sT.n)
                    sS.inc(s.activation(ffn16[:], psA[:, 0:1024], AF.Copy,
                                        bias=0.0, scale=1.0))

                @B.vector
                def _(v):
                    v.wait_ge(sS.h, sS.n)
                    sV.inc(v.scalar_tensor_tensor(dst_ap, ffn16[:], 0.5, src_ap,
                                                  OP.mult, OP.add))

                @B.gpsimd
                def _(g):
                    g.wait_ge(sV.h, sV.n)
                    sG.inc(g.tensor_add(dst_ap, dst_ap, hbias[:]), 1)

        mb_g = [0]
        mb_sL = {}
        mb_sT = {}
        for s_i in range(NSEQ):
            with nc.Block() as B:
                @B.sync
                def _(sync):
                    wait_all(sync)
                    for g4 in range(4):
                        out3 = R3[:, g4 * 4:(g4 + 1) * 4, :]
                        base = s_i * T * D + g4 * 4 * 128 * D
                        in3 = bass.AP(x_d.tensor, base, [[D, 128], [128 * D, 4], [1, D]])
                        sL.inc(sync.dma_start(out=out3, in_=in3), 16)

            ln_stats16()
            for i in range(TT):
                Ri = R[:, i * 1024:(i + 1) * 1024]
                ffn_core(Ri, Ri, mean16[:, i:i + 1], rstd16[:, i:i + 1])
            if upto == "ffn1":
                _copyout_R(nc, R, out_d, s_i, sG, wait_all)
                continue

            ln_stats16()
            for i in range(TT):
                Ri = R[:, i * 1024:(i + 1) * 1024]
                with nc.Block() as B:
                    @B.vector
                    def _(v):
                        v.wait_ge(sS.h, sS.n)
                        wait_all(v, sT)
                        sV.inc(v.tensor_scalar(h1[:], Ri, mean16[:, i:i + 1],
                                               rstd16[:, i:i + 1], OP.subtract, OP.mult))

                    @B.tensor
                    def _(t):
                        t.wait_ge(sV.h, sV.n)
                        for cc in range(DC):
                            bank, j = divmod(cc, 4)
                            r = t.matmul(psB[:, bank * 512 + j * 128:bank * 512 + (j + 1) * 128],
                                         h1[:, cc * 128:(cc + 1) * 128], id32[:],
                                         is_transpose=True, start=(j == 0), stop=(j == 3))
                        sT.inc(r)

                    @B.vector
                    def _(v):
                        v.wait_ge(sT.h, sT.n)
                        for bank in range(2):
                            pb3 = psB[:, bank * 512:(bank + 1) * 512].rearrange(
                                "p (s t) -> p s t", t=128)
                            dh = hfm3_h[:, bank * 4:bank * 4 + 4, i * 128:(i + 1) * 128]
                            dl = hfm3_l[:, bank * 4:bank * 4 + 4, i * 128:(i + 1) * 128]
                            v.tensor_copy(dh, pb3)
                            v.tensor_tensor(dl, pb3, dh, OP.subtract)
                        sV.inc(v.nop())

                    @B.gpsimd
                    def _(g):
                        g.wait_ge(sV.h, sV.n)
                        sG.inc(g.dma_start(out=rsp_d[i * 128:(i + 1) * 128, :], in_=Ri), 16)

            if upto == "ln2":
                with nc.Block() as B:
                    @B.gpsimd
                    def _(g):
                        wait_all(g)
                        sG.inc(g.dma_start(out=out_d[s_i * T:(s_i + 1) * T, :], in_=rsp_d[:]), 16)
                continue

            for dmc in range(DC):
                with nc.Block() as B:
                    @B.sync
                    def _(sync):
                        if dmc == 0:
                            wait_all(sync, sG, sV)
                        sync.wait_ge(sV.h, sV.n)
                        sL.inc(sync.dma_start(out=wvstage[:],
                                              in_=wv_d[dmc * 128:(dmc + 1) * 128, :]), 16)

                    @B.vector
                    def _(v):
                        v.wait_ge(sL.h, sL.n)
                        wh = wv_h[:, dmc * 1024:(dmc + 1) * 1024]
                        wl = wv_l[:, dmc * 1024:(dmc + 1) * 1024]
                        v.tensor_copy(wh, wvstage[:])
                        sV.inc(v.tensor_tensor(wl, wvstage[:], wh, OP.subtract))
            vp_evac = {}
            for g8 in range(8):
                ps = psA if g8 % 2 == 0 else psB
                with nc.Block() as B:
                    @B.tensor
                    def _(t):
                        t.wait_ge(sV.h, vp_evac[g8 - 2] if g8 >= 2 else sV.n)
                        for dmc in range(DC):
                            for kj in range(2):
                                kc = g8 * 2 + kj
                                lh = hfm_h[:, dmc * 2048 + kc * 128:dmc * 2048 + (kc + 1) * 128]
                                ll = hfm_l[:, dmc * 2048 + kc * 128:dmc * 2048 + (kc + 1) * 128]
                                for dh in range(2):
                                    pb = ps[:, kj * 1024 + dh * 512:kj * 1024 + (dh + 1) * 512]
                                    rh = wv_h[:, dmc * 1024 + dh * 512:dmc * 1024 + (dh + 1) * 512]
                                    rl = wv_l[:, dmc * 1024 + dh * 512:dmc * 1024 + (dh + 1) * 512]
                                    t.matmul(pb, lh, rh, start=(dmc == 0), stop=False)
                                    t.matmul(pb, lh, rl, start=False, stop=False)
                                    r = t.matmul(pb, ll, rh, start=False, stop=(dmc == DC - 1))
                        sT.inc(r)

                    @B.vector
                    def _(v):
                        v.wait_ge(sT.h, sT.n)
                        for kj in range(2):
                            kc = g8 * 2 + kj
                            for dh in range(2):
                                pb = ps[:, kj * 1024 + dh * 512:kj * 1024 + (dh + 1) * 512]
                                vh = v_h[:, kc * 1024 + dh * 512:kc * 1024 + (dh + 1) * 512]
                                vl = v_l[:, kc * 1024 + dh * 512:kc * 1024 + (dh + 1) * 512]
                                v.tensor_copy(vh, pb)
                                v.tensor_tensor(vl, pb, vh, OP.subtract)
                        r = v.nop()
                        sV.inc(r)
                        vp_evac[g8] = sV.n

            if upto == "vphase":
                with nc.Block() as B:
                    @B.gpsimd
                    def _(g):
                        wait_all(g)
                        for kc in range(16):
                            dst = out_d[s_i * T + kc * 128:s_i * T + (kc + 1) * 128, :]
                            sG.inc(g.dma_start(out=dst, in_=v_h[:, kc * 1024:(kc + 1) * 1024]), 16)
                continue

            for qb in range(NQB):
                for mi in range(2):
                    for ac in range(DC):
                        g = mb_g[0] + mi * DC + ac
                        buf = g % 2
                        with nc.Block() as B:
                            @B.sync
                            def _(sync):
                                sync.wait_ge(sT.h, mb_sT.get(g - 2, sT.n))
                                sL.inc(sync.dma_start(
                                    out=Mb_h[buf][:], in_=m_hbm[mi][0][ac * 128:(ac + 1) * 128, :]), 16)
                                sL.inc(sync.dma_start(
                                    out=Mb_l[buf][:], in_=m_hbm[mi][1][ac * 128:(ac + 1) * 128, :]), 16)
                                mb_sL[g] = sL.n

                            @B.tensor
                            def _(t):
                                t.wait_ge(sL.h, mb_sL[g])
                                if mi == 0 and ac == 0:
                                    t.wait_ge(sV.h, sV.n)
                                ps = psA if mi == 0 else psB
                                for bc in range(DC):
                                    pb = ps[:, (bc // 2) * 512 + (bc % 2) * 256:
                                            (bc // 2) * 512 + (bc % 2) * 256 + 256]
                                    rh = hfm_h[:, ac * 2048 + qb * 256:ac * 2048 + (qb + 1) * 256]
                                    rl = hfm_l[:, ac * 2048 + qb * 256:ac * 2048 + (qb + 1) * 256]
                                    lh = Mb_h[buf][:, bc * 128:(bc + 1) * 128]
                                    ll = Mb_l[buf][:, bc * 128:(bc + 1) * 128]
                                    st0 = (ac == 0 and bc % 2 == 0)
                                    sp0 = (ac == DC - 1 and bc % 2 == 1)
                                    t.matmul(pb, lh, rh, start=st0, stop=False,
                                             skip_group_check=True)
                                    t.matmul(pb, lh, rl, start=False, stop=False,
                                             skip_group_check=True)
                                    r = t.matmul(pb, ll, rh, start=False, stop=sp0,
                                                 skip_group_check=True)
                                sT.inc(r)
                                mb_sT[g] = sT.n
                mb_g[0] += 16
                with nc.Block() as B:
                    @B.vector
                    def _(v):
                        v.wait_ge(sT.h, sT.n)
                        for mi in range(2):
                            ps = psA if mi == 0 else psB
                            for bc in range(DC):
                                pb = ps[:, (bc // 2) * 512 + (bc % 2) * 256:
                                        (bc // 2) * 512 + (bc % 2) * 256 + 256]
                                th = tmp_h[mi][:, bc * 256:(bc + 1) * 256]
                                tl = tmp_l[mi][:, bc * 256:(bc + 1) * 256]
                                v.tensor_copy(th, pb)
                                v.tensor_tensor(tl, pb, th, OP.subtract)
                        sV.inc(v.nop())

                for qt in range(2):
                    qtile = qb * 2 + qt
                    with nc.Block() as B:
                        @B.tensor
                        def _(t):
                            t.wait_ge(sV.h, sV.n)
                            for mi in range(2):
                                ps = psA if mi == 0 else psB
                                for bc in range(DC):
                                    lh = tmp_h[mi][:, bc * 256 + qt * 128:bc * 256 + qt * 128 + 128]
                                    ll = tmp_l[mi][:, bc * 256 + qt * 128:bc * 256 + qt * 128 + 128]
                                    for kc in range(4):
                                        t.matmul(ps[:, kc * 512:(kc + 1) * 512], lh,
                                                 hfm_h[:, bc * 2048 + kc * 512:bc * 2048 + (kc + 1) * 512],
                                                 start=(bc == 0), stop=False)
                                    for kc in range(4):
                                        t.matmul(ps[:, kc * 512:(kc + 1) * 512], lh,
                                                 hfm_l[:, bc * 2048 + kc * 512:bc * 2048 + (kc + 1) * 512],
                                                 start=False, stop=False)
                                    for kc in range(4):
                                        r = t.matmul(ps[:, kc * 512:(kc + 1) * 512], ll,
                                                     hfm_h[:, bc * 2048 + kc * 512:bc * 2048 + (kc + 1) * 512],
                                                     start=False, stop=(bc == DC - 1))
                            sT.inc(r)

                        @B.vector
                        def _(v):
                            v.wait_ge(sT.h, sT.n)
                            v.reduce_max(r_nm1[:], psA[:], axis=AX.X, negate=True)
                            sV.inc(v.reduce_max(r_nm2[:], psB[:], axis=AX.X, negate=True))

                        @B.scalar
                        def _(s):
                            s.wait_ge(sV.h, sV.n)
                            s.activation(E1[:], psA[:], AF.Exp, bias=r_nm1[:, 0:1],
                                         scale=1.0, accum_out=r_z1[:, 0:1])
                            sS.inc(s.activation(psA[:], psB[:], AF.Exp, bias=r_nm2[:, 0:1],
                                                scale=1.0, accum_out=r_z2[:, 0:1]))

                        @B.vector
                        def _(v):
                            v.wait_ge(sS.h, sS.n)
                            v.reciprocal(r_t1[:], r_z2[:])
                            v.tensor_mul(r_t2[:], r_lamneg[:], r_z1[:])
                            v.tensor_mul(r_rq[:], r_z1[:], r_z1[:])
                            v.drain()
                            v.tensor_mul(r_srow[:], r_t2[:], r_t1[:])
                            v.tensor_scalar_mul(r_epsZ[:], r_rq[:], EPS)
                            v.drain()
                            sV.inc(v.scalar_tensor_tensor(Dm[:], psA[:], r_srow[:, 0:1],
                                                          E1[:], OP.mult, OP.add))

                        @B.tensor
                        def _(t):
                            t.wait_ge(sV.h, sV.n)
                            for kc in range(16):
                                bank, j = divmod(kc, 4)
                                r = t.matmul(psB[:, bank * 512 + j * 128:bank * 512 + (j + 1) * 128],
                                             Dm[:, kc * 128:(kc + 1) * 128], id32[:],
                                             is_transpose=True, start=(j == 0), stop=(j == 3))
                            sT.inc(r)

                        @B.vector
                        def _(v):
                            v.wait_ge(sT.h, sT.n)
                            v.tensor_copy(PT_h[:], psB[:])
                            sV.inc(v.tensor_tensor(PT_l[:], psB[:], PT_h[:], OP.subtract))

                    with nc.Block() as B:
                        @B.sync
                        def _(sync):
                            sync.wait_ge(sV.h, sV.n)
                            sL.inc(sync.dma_start(out=Rio[:],
                                                  in_=rsp_d[qtile * 128:(qtile + 1) * 128, :]), 16)

                        @B.tensor
                        def _(t):
                            t.wait_ge(sV.h, sV.n)
                            for kc in range(16):
                                lh = PT_h[:, kc * 128:(kc + 1) * 128]
                                ll = PT_l[:, kc * 128:(kc + 1) * 128]
                                for dh in range(2):
                                    pb = psA[:, dh * 512:(dh + 1) * 512]
                                    rh = v_h[:, kc * 1024 + dh * 512:kc * 1024 + (dh + 1) * 512]
                                    rl = v_l[:, kc * 1024 + dh * 512:kc * 1024 + (dh + 1) * 512]
                                    t.matmul(pb, lh, rh, start=(kc == 0), stop=False)
                                    t.matmul(pb, lh, rl, start=False, stop=False)
                                    r = t.matmul(pb, ll, rh, start=False, stop=(kc == 15))
                            sT.inc(r)

                        @B.vector
                        def _(v):
                            v.wait_ge(sT.h, sT.n)
                            v.wait_ge(sL.h, sL.n)
                            v.wait_ge(sG.h, sG.n)
                            v.bn_stats(st4[:, 0:6], psA[:, 0:512])
                            v.bn_stats(st4[:, 6:12], psA[:, 512:1024])
                            v.drain()
                            v.tensor_mul(junk4[:], st4[:, 1:12:3], st4[:, 1:12:3])
                            v.reduce_sum(r_var[:], st4[:, 2:12:3], axis=AX.X)
                            v.drain()
                            v.reduce_sum(r_t1[:], junk4[:], axis=AX.X)
                            v.tensor_scalar_mul(r_t2[:], r_var[:], R1024)
                            v.drain()
                            v.scalar_tensor_tensor(r_msq[:], r_t1[:], 0.25, r_t2[:],
                                                   OP.mult, OP.add)
                            sV.inc(v.drain())

                        @B.scalar
                        def _(s):
                            s.wait_ge(sV.h, sV.n)
                            s.activation(r_l[:], r_msq[:], AF.Ln, bias=r_epsZ[:, 0:1],
                                         scale=1.0)
                            sS.inc(s.activation(r_trow[:], r_l[:], AF.Exp,
                                                bias=r_ln02[:, 0:1], scale=r_mhalf[:, 0:1]))

                        @B.vector
                        def _(v):
                            v.wait_ge(sS.h, sS.n)
                            sV.inc(v.scalar_tensor_tensor(Rio2[:], psA[:, 0:1024],
                                                          r_trow[:, 0:1], Rio[:],
                                                          OP.mult, OP.add))

                        @B.gpsimd
                        def _(g):
                            g.wait_ge(sV.h, sV.n)
                            sG.inc(g.dma_start(out=rsp2_d[qtile * 128:(qtile + 1) * 128, :],
                                               in_=Rio2[:]), 16)

            if upto == "attn":
                with nc.Block() as B:
                    @B.gpsimd
                    def _(g):
                        wait_all(g)
                        sG.inc(g.dma_start(out=out_d[s_i * T:(s_i + 1) * T, :], in_=rsp2_d[:]), 16)
                continue

            for i in range(TT):
                with nc.Block() as B:
                    @B.sync
                    def _(sync):
                        wait_all(sync, sG, sV)
                        sL.inc(sync.dma_start(out=Rio[:],
                                              in_=rsp2_d[i * 128:(i + 1) * 128, :]), 16)
                ln_tile_single(Rio[:])
                ffn_core(Rio[:], Rio2[:], r_mean[:, 0:1], r_rstd[:, 0:1])
                with nc.Block() as B:
                    @B.gpsimd
                    def _(g):
                        g.wait_ge(sV.h, sV.n)
                        dst = out_d[s_i * T + i * 128:s_i * T + (i + 1) * 128, :]
                        sG.inc(g.dma_start(out=dst, in_=Rio2[:]), 16)

        _finish(nc, ALL)
    return nc


def _finish(nc, sems):
    with nc.Block() as B:
        @B.gpsimd
        def _(g):
            for s in sems:
                if s.n:
                    g.wait_ge(s.h, s.n)


def _copyout_R(nc, R, out_d, s_i, sG, wait_all):
    with nc.Block() as B:
        @B.gpsimd
        def _(g):
            wait_all(g)
            for i in range(TT):
                dst = out_d[s_i * T + i * 128:s_i * T + (i + 1) * 128, :]
                sG.inc(g.dma_start(out=dst, in_=R[:, i * 1024:(i + 1) * 1024]), 16)


_NC_CACHE = {}


def _get_program():
    if "nc" not in _NC_CACHE:
        _NC_CACHE["nc"] = build_program()
    return _NC_CACHE["nc"]


def kernel(L0_attn_x, L0_attn_w_q12, L0_attn_w_k12, L0_attn_w_v,
           L0_attn_lambda_q1, L0_attn_lambda_k1, L0_attn_lambda_q2,
           L0_attn_lambda_k2, L0_ffn_weight, L0_ffn_bias):
    x = np.ascontiguousarray(np.asarray(L0_attn_x, dtype=np.float32))
    B, S, d = x.shape
    ncore = 8
    shards = x.reshape(ncore, (B // ncore) * S, d)

    common = {
        "wq12": np.ascontiguousarray(L0_attn_w_q12, dtype=np.float32),
        "wk12": np.ascontiguousarray(L0_attn_w_k12, dtype=np.float32),
        "wv": np.ascontiguousarray(L0_attn_w_v, dtype=np.float32),
        "fw": np.ascontiguousarray(L0_ffn_weight, dtype=np.float32),
        "fb": np.ascontiguousarray(L0_ffn_bias, dtype=np.float32),
        "lq1": np.ascontiguousarray(L0_attn_lambda_q1, dtype=np.float32),
        "lk1": np.ascontiguousarray(L0_attn_lambda_k1, dtype=np.float32),
        "lq2": np.ascontiguousarray(L0_attn_lambda_q2, dtype=np.float32),
        "lk2": np.ascontiguousarray(L0_attn_lambda_k2, dtype=np.float32),
        "id32": np.eye(128, dtype=np.float32),
        "id8": np.eye(128, dtype=np.float32).astype(ml_dtypes.float8_e4m3),
    }
    in_maps = [dict(common, x=np.ascontiguousarray(shards[c])) for c in range(ncore)]
    nc = _get_program()
    res = run_bass_kernel_spmd(nc, in_maps, core_ids=list(range(ncore)))
    out = np.concatenate([r["out"] for r in res.results], axis=0)
    return out.reshape(B, S, d)


# version 03 (final)
# speedup vs baseline: 1.0318x; 1.0318x over previous
import contextlib
import numpy as np
import ml_dtypes
import concourse.bass as bass
import concourse.mybir as mybir
from concourse.bass_utils import run_bass_kernel_spmd
from concourse.alu_op_type import AluOpType as OP

F32 = mybir.dt.float32
F16 = mybir.dt.float16
BF16 = mybir.dt.bfloat16
FP8 = mybir.dt.float8e4
AF = mybir.ActivationFunctionType
AX = mybir.AxisListType

D = 1024
T = 2048
DC = D // 128
TT = T // 128
NSEQ = 2
QB = 256
NQB = T // QB
SCALE = float(D) ** (-0.25)
EPS = 1e-5
LAM0 = 0.8
R1024 = 1.0 / 1024.0
LN02 = float(np.log(0.2))


class Sem:
    def __init__(self, E, nc, name):
        self.h = E(nc.semaphore(name))
        self.n = 0

    def inc(self, res, k=1):
        res.then_inc(self.h, k)
        self.n += k


def build_program(upto=None):
    nc = bass.Bass("TRN2", target_bir_lowering=False)

    x_d = nc.dram_tensor("x", [NSEQ * T, D], F32, kind="ExternalInput").ap()
    wq12_d = nc.dram_tensor("wq12", [D, 2 * D], F32, kind="ExternalInput").ap()
    wk12_d = nc.dram_tensor("wk12", [D, 2 * D], F32, kind="ExternalInput").ap()
    wv_d = nc.dram_tensor("wv", [D, D], F32, kind="ExternalInput").ap()
    fw_d = nc.dram_tensor("fw", [D, D], F32, kind="ExternalInput").ap()
    fb_d = nc.dram_tensor("fb", [D], F32, kind="ExternalInput").ap()
    lq1_d = nc.dram_tensor("lq1", [D], F32, kind="ExternalInput").ap()
    lk1_d = nc.dram_tensor("lk1", [D], F32, kind="ExternalInput").ap()
    lq2_d = nc.dram_tensor("lq2", [D], F32, kind="ExternalInput").ap()
    lk2_d = nc.dram_tensor("lk2", [D], F32, kind="ExternalInput").ap()
    id32_d = nc.dram_tensor("id32", [128, 128], F32, kind="ExternalInput").ap()
    id8_d = nc.dram_tensor("id8", [128, 128], FP8, kind="ExternalInput").ap()
    out_d = nc.dram_tensor("out", [NSEQ * T, D], F32, kind="ExternalOutput").ap()
    m_hbm = [[nc.dram_tensor(f"m{mi}{p}", [D, D], BF16).ap() for p in ("h", "l")]
             for mi in (1, 2)]
    rsp_d = nc.dram_tensor("rsp", [T, D], F32).ap()
    rsp2_d = nc.dram_tensor("rsp2", [T, D], F32).ap()

    with contextlib.ExitStack() as ES:
        E = ES.enter_context
        AR_BYTES = 212224
        arena = E(nc.sbuf_tensor("arena", [128, AR_BYTES // 4], F32))

        def fview(off_b, words):
            return arena[:, off_b // 4: off_b // 4 + words]

        def bview(off_b, elems):
            return arena[:, off_b // 4: off_b // 4 + (elems + 1) // 2].bitcast(BF16)[:, :elems]

        def qview(off_b, elems):
            return arena[:, off_b // 4: off_b // 4 + (elems + 3) // 4].bitcast(FP8)[:, :elems]

        def hview(off_b, elems):
            return arena[:, off_b // 4: off_b // 4 + (elems + 1) // 2].bitcast(F16)[:, :elems]

        cur = [0]

        def alloc(nbytes, align=64):
            o = (cur[0] + align - 1) // align * align
            cur[0] = o + nbytes
            assert cur[0] <= AR_BYTES, f"arena overflow {cur[0]}"
            return o

        o_wqT = alloc(16384)
        o_hbias = alloc(4096)
        o_id32 = alloc(512)
        o_id8 = alloc(128)
        o_rows = alloc(512)
        o_st = alloc(2048)
        o_H = alloc(65536)
        o_RV = alloc(65536)
        o_tmp = alloc(16384)
        o_E1 = alloc(8192)
        o_Dm = alloc(8192)
        o_PT = alloc(8192)
        o_Mbuf = alloc(4096)
        o_Mbuf2 = alloc(4096)
        o_Rio = alloc(4096)
        o_Rio2 = alloc(4096)

        wqT = bview(o_wqT, 8192)
        wqT3 = wqT.rearrange("p (cc f) -> p cc f", f=1024)
        hbias = fview(o_hbias, 1024)
        id32 = fview(o_id32, 128)
        id8 = qview(o_id8, 128)

        rowsv = fview(o_rows, 128)

        def row(i):
            return rowsv[:, i:i + 1]

        r_nm1, r_nm2, r_z1, r_z2 = row(4), row(5), row(6), row(7)
        r_srow, r_epsZ, r_msq = row(8), row(9), row(10)
        r_t1, r_t2, r_eps, r_lamneg = row(12), row(13), row(14), row(15)
        r_trow, r_l, r_e1, r_e2 = row(16), row(17), row(18), row(19)
        r_zero, r_mhalf, r_ln02, r_rq = row(21), row(22), row(23), row(24)
        r_sum, r_mean, r_var, r_rstd, r_a, r_mm, r_u0 = (
            row(25), row(26), row(27), row(28), row(29), row(30), row(31))

        stB = fview(o_st, 512)
        stB3 = stB[:, 0:192].rearrange("p (i s) -> p i s", s=12)
        junk64 = fview(o_st + 768, 64)
        junk64_3 = junk64.rearrange("p (i s) -> p i s", s=4)
        sum16 = fview(o_st + 1024, 16)
        cv16 = fview(o_st + 1088, 16)
        m2s16 = fview(o_st + 1152, 16)
        mean16 = fview(o_st + 1216, 16)
        cvn16 = fview(o_st + 1280, 16)
        a16 = fview(o_st + 1344, 16)
        mm16 = fview(o_st + 1408, 16)
        u16 = fview(o_st + 1472, 16)
        u16b = fview(o_st + 1536, 16)
        l16 = fview(o_st + 1600, 16)
        rstd16 = fview(o_st + 1664, 16)
        eps16 = fview(o_st + 1728, 16)
        st4 = fview(o_st, 512)[:, 0:12]
        junk4 = fview(o_st + 768, 64)[:, 48:52]

        hfm_h = bview(o_H, 16384)
        hfm_l = bview(o_H + 32768, 16384)
        hfm3_h = hfm_h.rearrange("p (cc t) -> p cc t", t=2048)
        hfm3_l = hfm_l.rearrange("p (cc t) -> p cc t", t=2048)
        wqt_h = bview(o_H, 8192)
        wqt_l = bview(o_H + 16384, 8192)
        wkt_h = bview(o_H + 32768, 8192)
        wkt_l = bview(o_H + 49152, 8192)
        wqt3 = [(wqt_h.rearrange("p (ec a) -> p ec a", a=1024),
                 wqt_l.rearrange("p (ec a) -> p ec a", a=1024)),
                (wkt_h.rearrange("p (ec a) -> p ec a", a=1024),
                 wkt_l.rearrange("p (ec a) -> p ec a", a=1024))]

        R = fview(o_RV, 16384)
        R3 = R.rearrange("p (i c) -> p i c", c=1024)
        v_h = bview(o_RV, 16384)
        v_l = bview(o_RV + 32768, 16384)
        wstage = fview(o_RV, 1024)
        wstage2 = fview(o_RV + 4096, 1024)
        mst_h = bview(o_RV + 8192, 512)
        mst_l = bview(o_RV + 10240, 512)
        qa_bf = bview(o_RV + 12288, 512)
        qb_bf = bview(o_RV + 14336, 512)
        qmask = fview(o_RV + 16384, 512)
        q8st = qview(o_RV + 18432, 512)
        qsel32 = fview(o_RV + 20480, 512)
        lam_bc = [fview(o_RV + 24576 + i * 4096, 1024) for i in range(4)]

        tmp_h = [bview(o_tmp, 2048), bview(o_tmp + 8192, 2048)]
        tmp_l = [bview(o_tmp + 4096, 2048), bview(o_tmp + 12288, 2048)]
        E1 = fview(o_E1, 2048)
        Dm = fview(o_Dm, 2048)
        PT_h = bview(o_PT, 2048)
        PT_l = bview(o_PT + 4096, 2048)
        Mb_h = [bview(o_Mbuf, 1024), bview(o_Mbuf2, 1024)]
        Mb_l = [bview(o_Mbuf + 2048, 1024), bview(o_Mbuf2 + 2048, 1024)]
        Rio = fview(o_Rio, 1024)
        Rio2 = fview(o_Rio2, 1024)
        wv_h = bview(o_tmp, 8192)
        wv_l = bview(o_E1, 8192)
        h1 = fview(o_PT, 1024)
        q8t = qview(o_PT + 4096, 1024)
        q8T = bview(o_PT + 5120, 1024)
        ffn16 = hview(o_Mbuf, 1024)
        wvstage = fview(o_Rio, 1024)

        psA = E(nc.psum_tensor("psA", [128, 2048], F32))
        psB = E(nc.psum_tensor("psB", [128, 2048], F32))
        psB8 = psB.bitcast(FP8)

        sL = Sem(E, nc, "sL")
        sG = Sem(E, nc, "sG")
        sV = Sem(E, nc, "sV")
        sS = Sem(E, nc, "sS")
        sT = Sem(E, nc, "sT")
        sMb = Sem(E, nc, "sMb")
        ALL = (sL, sG, sV, sS, sT)

        def wait_all(eng, *sems):
            for s in (sems or ALL):
                if s.n:
                    eng.wait_ge(s.h, s.n)

        with nc.Block() as B:
            @B.sync
            def _(sync):
                sL.inc(sync.dma_start(out=id32[:], in_=id32_d[:]), 16)
                sL.inc(sync.dma_start(out=id8[:], in_=id8_d[:]), 16)

            @B.gpsimd
            def _(g):
                for i, dd in enumerate((lq1_d, lk1_d, lq2_d, lk2_d)):
                    sG.inc(g.dma_start(out=lam_bc[i][:],
                                       in_=dd.unsqueeze(0).partition_broadcast(128)), 16)
                sG.inc(g.dma_start(out=wstage2[:],
                                   in_=fb_d.unsqueeze(0).partition_broadcast(128)), 16)

            @B.vector
            def _(v):
                v.memset(r_eps[:], EPS)
                v.memset(r_zero[:], 0.0)
                v.memset(r_mhalf[:], -0.5)
                v.memset(r_ln02[:], LN02)
                v.memset(eps16[:], EPS)
                v.wait_ge(sG.h, sG.n)
                v.tensor_mul(lam_bc[0][:], lam_bc[0][:], lam_bc[1][:])
                v.tensor_mul(lam_bc[2][:], lam_bc[2][:], lam_bc[3][:])
                v.reduce_sum(r_t1[:], lam_bc[0][:], axis=AX.X)
                v.reduce_sum(r_t2[:], lam_bc[2][:], axis=AX.X)
                v.drain()
                sV.inc(v.tensor_scalar_mul(hbias[:], wstage2[:], 0.5))

            @B.scalar
            def _(s):
                s.wait_ge(sV.h, sV.n)
                s.activation(r_e1[:], r_t1[:], AF.Exp, bias=r_zero[:, 0:1], scale=1.0)
                sS.inc(s.activation(r_e2[:], r_t2[:], AF.Exp, bias=r_zero[:, 0:1], scale=1.0))

            @B.vector
            def _(v):
                v.wait_ge(sS.h, sS.n)
                v.tensor_sub(r_t1[:], r_e2[:], r_e1[:])
                v.drain()
                v.tensor_single_scalar(r_lamneg[:], r_t1[:], -LAM0, OP.add)
                sV.inc(v.drain())

        for fc in range(DC):
            with nc.Block() as B:
                @B.sync
                def _(sync):
                    sync.wait_ge(sV.h, sV.n)
                    sL.inc(sync.dma_start(out=wstage[:],
                                          in_=fw_d[fc * 128:(fc + 1) * 128, :]), 16)

                @B.vector
                def _(v):
                    v.wait_ge(sL.h, sL.n)
                    sV.inc(v.tensor_scalar(wstage2[:], wstage[:], -448.0, 448.0,
                                           OP.max, OP.min))

                @B.tensor
                def _(t):
                    t.wait_ge(sV.h, sV.n)
                    for cc in range(DC):
                        bank, j = divmod(cc, 4)
                        r = t.matmul(psB[:, bank * 512 + j * 128:bank * 512 + (j + 1) * 128],
                                     wstage2[:, cc * 128:(cc + 1) * 128], id32[:],
                                     is_transpose=True, start=(j == 0), stop=(j == 3))
                    sT.inc(r)

                @B.vector
                def _(v):
                    v.wait_ge(sT.h, sT.n)
                    for bank in range(2):
                        pb = psB[:, bank * 512:(bank + 1) * 512]
                        v.tensor_copy(q8st[:], pb)
                        v.tensor_copy(qa_bf[:], q8st[:])
                        v.tensor_single_scalar(q8st[:], pb, 0.5, OP.mult)
                        v.tensor_single_scalar(qb_bf[:], q8st[:], 2.0, OP.mult)
                        v.tensor_single_scalar(qmask[:], pb, 1.0, OP.is_lt)
                        v.tensor_single_scalar(qsel32[:], pb, -1.0, OP.is_gt)
                        v.drain()
                        v.tensor_mul(qmask[:], qmask[:], qsel32[:])
                        v.tensor_sub(qsel32[:], qa_bf[:], qb_bf[:])
                        v.drain()
                        v.tensor_mul(qsel32[:], qsel32[:], qmask[:])
                        v.drain()
                        dst = wqT3[:, bank * 4:bank * 4 + 4, fc * 128:(fc + 1) * 128]
                        v.tensor_tensor(dst, qb_bf[:].rearrange("p (s f) -> p s f", f=128),
                                        qsel32[:].rearrange("p (s f) -> p s f", f=128),
                                        OP.add)
                    sV.inc(v.drain())

        for mi in range(2):
            for wi, wd in enumerate((wq12_d, wk12_d)):
                th3, tl3 = wqt3[wi]
                for dc in range(DC):
                    with nc.Block() as B:
                        @B.sync
                        def _(sync):
                            sync.wait_ge(sV.h, sV.n)
                            sync.wait_ge(sT.h, sT.n)
                            sL.inc(sync.dma_start(
                                out=wstage[:],
                                in_=wd[dc * 128:(dc + 1) * 128, mi * D:(mi + 1) * D]), 16)

                        @B.tensor
                        def _(t):
                            t.wait_ge(sL.h, sL.n)
                            t.wait_ge(sV.h, sV.n)
                            for ec in range(DC):
                                bank, j = divmod(ec, 4)
                                r = t.matmul(
                                    psB[:, bank * 512 + j * 128:bank * 512 + (j + 1) * 128],
                                    wstage[:, ec * 128:(ec + 1) * 128], id32[:],
                                    is_transpose=True, start=(j == 0), stop=(j == 3))
                            sT.inc(r)

                        @B.vector
                        def _(v):
                            v.wait_ge(sT.h, sT.n)
                            for bank in range(2):
                                pb3 = psB[:, bank * 512:(bank + 1) * 512].rearrange(
                                    "p (s a) -> p s a", a=128)
                                dh = th3[:, bank * 4:bank * 4 + 4, dc * 128:(dc + 1) * 128]
                                dl = tl3[:, bank * 4:bank * 4 + 4, dc * 128:(dc + 1) * 128]
                                v.tensor_copy(dh, pb3)
                                v.tensor_tensor(dl, pb3, dh, OP.subtract)
                            sV.inc(v.nop())
            for ac in range(DC):
                for bh in range(2):
                    with nc.Block() as B:
                        @B.tensor
                        def _(t):
                            t.wait_ge(sV.h, sV.n)
                            t.wait_ge(sG.h, sG.n)
                            pb = psA[:, 0:512]
                            for ec in range(DC):
                                lh = wqt_h[:, ec * 1024 + ac * 128:ec * 1024 + (ac + 1) * 128]
                                ll = wqt_l[:, ec * 1024 + ac * 128:ec * 1024 + (ac + 1) * 128]
                                rh = wkt_h[:, ec * 1024 + bh * 512:ec * 1024 + (bh + 1) * 512]
                                rl = wkt_l[:, ec * 1024 + bh * 512:ec * 1024 + (bh + 1) * 512]
                                t.matmul(pb, lh, rh, start=(ec == 0), stop=False)
                                t.matmul(pb, lh, rl, start=False, stop=False)
                                r = t.matmul(pb, ll, rh, start=False, stop=(ec == DC - 1))
                            sT.inc(r)

                        @B.vector
                        def _(v):
                            v.wait_ge(sT.h, sT.n)
                            pb = psA[:, 0:512]
                            v.tensor_single_scalar(mst_h[:], pb, SCALE, OP.mult)
                            sV.inc(v.scalar_tensor_tensor(mst_l[:], pb, SCALE, mst_h[:],
                                                          OP.mult, OP.subtract))

                        @B.gpsimd
                        def _(g):
                            g.wait_ge(sV.h, sV.n)
                            dh = m_hbm[mi][0][ac * 128:(ac + 1) * 128, bh * 512:(bh + 1) * 512]
                            dl = m_hbm[mi][1][ac * 128:(ac + 1) * 128, bh * 512:(bh + 1) * 512]
                            sG.inc(g.dma_start(out=dh, in_=mst_h[:]), 16)
                            sG.inc(g.dma_start(out=dl, in_=mst_l[:]), 16)

        if upto == "prologue":
            with nc.Block() as B:
                @B.gpsimd
                def _(g):
                    wait_all(g)
                    sG.inc(g.dma_start(out=out_d[0:1024, :], in_=m_hbm[0][0][:]), 16)
                    sG.inc(g.dma_start(out=out_d[1024:2048, :], in_=m_hbm[0][1][:]), 16)
                    sG.inc(g.dma_start(out=out_d[2048:3072, :], in_=m_hbm[1][0][:]), 16)
                    sG.inc(g.dma_start(out=out_d[3072:4096, :], in_=m_hbm[1][1][:]), 16)
            _finish(nc, ALL)
            return nc

        def ln_stats16():
            with nc.Block() as B:
                @B.vector
                def _(v):
                    wait_all(v, sS, sT, sG, sL)
                    for i in range(TT):
                        v.bn_stats(stB3[:, i, 0:6], R[:, i * 1024:i * 1024 + 512])
                        v.bn_stats(stB3[:, i, 6:12], R[:, i * 1024 + 512:(i + 1) * 1024])
                    v.drain()
                    means3 = stB3[:, :, 1:12:3]
                    cvars3 = stB3[:, :, 2:12:3]
                    v.tensor_tensor(junk64_3, means3, means3, OP.mult)
                    v.reduce_sum(sum16[:], means3, axis=AX.X)
                    v.reduce_sum(cv16[:], cvars3, axis=AX.X)
                    v.drain()
                    v.reduce_sum(m2s16[:], junk64_3, axis=AX.X)
                    v.tensor_scalar_mul(mean16[:], sum16[:], 0.25)
                    v.tensor_scalar_mul(cvn16[:], cv16[:], R1024)
                    v.drain()
                    v.scalar_tensor_tensor(a16[:], m2s16[:], 0.25, cvn16[:],
                                           OP.mult, OP.add)
                    v.tensor_mul(mm16[:], mean16[:], mean16[:])
                    v.drain()
                    v.scalar_tensor_tensor(u16[:], mm16[:], -1.0, a16[:],
                                           OP.mult, OP.add)
                    v.drain()
                    v.tensor_add(u16b[:], u16[:], eps16[:])
                    sV.inc(v.drain())

                @B.scalar
                def _(s):
                    s.wait_ge(sV.h, sV.n)
                    s.activation(l16[:], u16b[:], AF.Ln, bias=r_zero[:, 0:1], scale=1.0)
                    sS.inc(s.activation(rstd16[:], l16[:], AF.Exp, bias=r_zero[:, 0:1],
                                        scale=r_mhalf[:, 0:1]))

        def ln_tile_single(src_ap):
            with nc.Block() as B:
                @B.vector
                def _(v):
                    wait_all(v, sS, sT, sG, sL)
                    v.bn_stats(st4[:, 0:6], src_ap[:, 0:512])
                    v.bn_stats(st4[:, 6:12], src_ap[:, 512:1024])
                    v.drain()
                    v.tensor_mul(junk4[:], st4[:, 1:12:3], st4[:, 1:12:3])
                    v.reduce_sum(r_sum[:], st4[:, 1:12:3], axis=AX.X)
                    v.reduce_sum(r_var[:], st4[:, 2:12:3], axis=AX.X)
                    v.drain()
                    v.reduce_sum(r_t1[:], junk4[:], axis=AX.X)
                    v.tensor_scalar_mul(r_mean[:], r_sum[:], 0.25)
                    v.tensor_scalar_mul(r_t2[:], r_var[:], R1024)
                    v.drain()
                    v.scalar_tensor_tensor(r_a[:], r_t1[:], 0.25, r_t2[:],
                                           OP.mult, OP.add)
                    v.tensor_mul(r_mm[:], r_mean[:], r_mean[:])
                    v.drain()
                    v.scalar_tensor_tensor(r_u0[:], r_mm[:], -1.0, r_a[:],
                                           OP.mult, OP.add)
                    sV.inc(v.drain())

                @B.scalar
                def _(s):
                    s.wait_ge(sV.h, sV.n)
                    s.activation(r_l[:], r_u0[:], AF.Ln, bias=r_eps[:, 0:1], scale=1.0)
                    sS.inc(s.activation(r_rstd[:], r_l[:], AF.Exp, bias=r_zero[:, 0:1],
                                        scale=r_mhalf[:, 0:1]))

        def ffn_core(src_ap, dst_ap, mean_ap, rstd_ap):
            with nc.Block() as B:
                @B.vector
                def _(v):
                    v.wait_ge(sS.h, sS.n)
                    wait_all(v, sT)
                    sV.inc(v.tensor_scalar(q8t[:], src_ap, mean_ap, rstd_ap,
                                           OP.subtract, OP.mult))

                @B.tensor
                def _(t):
                    t.wait_ge(sV.h, sV.n)
                    for cc in range(DC):
                        r = t.matmul(psB8[:, cc * 256:cc * 256 + 256:2],
                                     q8t[:, cc * 128:(cc + 1) * 128], id8[:],
                                     is_transpose=True, start=(cc == 0), stop=(cc == 7))
                    sT.inc(r)

                @B.vector
                def _(v):
                    v.wait_ge(sT.h, sT.n)
                    src3 = psB8[:, 0:2048].rearrange("p (s t) -> p s t", t=256)[:, :, ::2]
                    sV.inc(v.tensor_copy(q8T[:].rearrange("p (s t) -> p s t", t=128), src3))

                @B.tensor
                def _(t):
                    t.wait_ge(sV.h, sV.n)
                    for fh in range(2):
                        pb = psA[:, fh * 512:(fh + 1) * 512]
                        for cc in range(DC):
                            r = t.matmul(pb, q8T[:, cc * 128:(cc + 1) * 128],
                                         wqT[:, cc * 1024 + fh * 512:cc * 1024 + (fh + 1) * 512],
                                         start=(cc == 0), stop=(cc == DC - 1))
                    sT.inc(r)

                @B.scalar
                def _(s):
                    s.wait_ge(sT.h, sT.n)
                    sS.inc(s.activation(ffn16[:], psA[:, 0:1024], AF.Copy,
                                        bias=0.0, scale=1.0))

                @B.vector
                def _(v):
                    v.wait_ge(sS.h, sS.n)
                    sV.inc(v.scalar_tensor_tensor(dst_ap, ffn16[:], 0.5, src_ap,
                                                  OP.mult, OP.add))

                @B.gpsimd
                def _(g):
                    g.wait_ge(sV.h, sV.n)
                    sG.inc(g.tensor_add(dst_ap, dst_ap, hbias[:]), 1)

        mb_g = [0]
        mb_sL = {}
        mb_sT = {}
        for s_i in range(NSEQ):
            pipe = {"ptev": None, "stt": None}
            with nc.Block() as B:
                @B.sync
                def _(sync):
                    wait_all(sync)
                    for g4 in range(4):
                        out3 = R3[:, g4 * 4:(g4 + 1) * 4, :]
                        base = s_i * T * D + g4 * 4 * 128 * D
                        in3 = bass.AP(x_d.tensor, base, [[D, 128], [128 * D, 4], [1, D]])
                        sL.inc(sync.dma_start(out=out3, in_=in3), 16)

            ln_stats16()
            for i in range(TT):
                Ri = R[:, i * 1024:(i + 1) * 1024]
                ffn_core(Ri, Ri, mean16[:, i:i + 1], rstd16[:, i:i + 1])
            if upto == "ffn1":
                _copyout_R(nc, R, out_d, s_i, sG, wait_all)
                continue

            ln_stats16()
            for i in range(TT):
                Ri = R[:, i * 1024:(i + 1) * 1024]
                with nc.Block() as B:
                    @B.vector
                    def _(v):
                        v.wait_ge(sS.h, sS.n)
                        wait_all(v, sT)
                        sV.inc(v.tensor_scalar(h1[:], Ri, mean16[:, i:i + 1],
                                               rstd16[:, i:i + 1], OP.subtract, OP.mult))

                    @B.tensor
                    def _(t):
                        t.wait_ge(sV.h, sV.n)
                        for cc in range(DC):
                            bank, j = divmod(cc, 4)
                            r = t.matmul(psB[:, bank * 512 + j * 128:bank * 512 + (j + 1) * 128],
                                         h1[:, cc * 128:(cc + 1) * 128], id32[:],
                                         is_transpose=True, start=(j == 0), stop=(j == 3))
                        sT.inc(r)

                    @B.vector
                    def _(v):
                        v.wait_ge(sT.h, sT.n)
                        for bank in range(2):
                            pb3 = psB[:, bank * 512:(bank + 1) * 512].rearrange(
                                "p (s t) -> p s t", t=128)
                            dh = hfm3_h[:, bank * 4:bank * 4 + 4, i * 128:(i + 1) * 128]
                            dl = hfm3_l[:, bank * 4:bank * 4 + 4, i * 128:(i + 1) * 128]
                            v.tensor_copy(dh, pb3)
                            v.tensor_tensor(dl, pb3, dh, OP.subtract)
                        sV.inc(v.nop())

                    @B.gpsimd
                    def _(g):
                        g.wait_ge(sV.h, sV.n)
                        sG.inc(g.dma_start(out=rsp_d[i * 128:(i + 1) * 128, :], in_=Ri), 16)

            if upto == "ln2":
                with nc.Block() as B:
                    @B.gpsimd
                    def _(g):
                        wait_all(g)
                        sG.inc(g.dma_start(out=out_d[s_i * T:(s_i + 1) * T, :], in_=rsp_d[:]), 16)
                continue

            for dmc in range(DC):
                with nc.Block() as B:
                    @B.sync
                    def _(sync):
                        if dmc == 0:
                            wait_all(sync, sG, sV)
                        sync.wait_ge(sV.h, sV.n)
                        sL.inc(sync.dma_start(out=wvstage[:],
                                              in_=wv_d[dmc * 128:(dmc + 1) * 128, :]), 16)

                    @B.vector
                    def _(v):
                        v.wait_ge(sL.h, sL.n)
                        wh = wv_h[:, dmc * 1024:(dmc + 1) * 1024]
                        wl = wv_l[:, dmc * 1024:(dmc + 1) * 1024]
                        v.tensor_copy(wh, wvstage[:])
                        sV.inc(v.tensor_tensor(wl, wvstage[:], wh, OP.subtract))
            vp_evac = {}
            for g8 in range(8):
                ps = psA if g8 % 2 == 0 else psB
                with nc.Block() as B:
                    @B.tensor
                    def _(t):
                        t.wait_ge(sV.h, vp_evac[g8 - 2] if g8 >= 2 else sV.n)
                        for dmc in range(DC):
                            for kj in range(2):
                                kc = g8 * 2 + kj
                                lh = hfm_h[:, dmc * 2048 + kc * 128:dmc * 2048 + (kc + 1) * 128]
                                ll = hfm_l[:, dmc * 2048 + kc * 128:dmc * 2048 + (kc + 1) * 128]
                                for dh in range(2):
                                    pb = ps[:, kj * 1024 + dh * 512:kj * 1024 + (dh + 1) * 512]
                                    rh = wv_h[:, dmc * 1024 + dh * 512:dmc * 1024 + (dh + 1) * 512]
                                    rl = wv_l[:, dmc * 1024 + dh * 512:dmc * 1024 + (dh + 1) * 512]
                                    t.matmul(pb, lh, rh, start=(dmc == 0), stop=False)
                                    t.matmul(pb, lh, rl, start=False, stop=False)
                                    r = t.matmul(pb, ll, rh, start=False, stop=(dmc == DC - 1))
                        sT.inc(r)

                    @B.vector
                    def _(v):
                        v.wait_ge(sT.h, sT.n)
                        for kj in range(2):
                            kc = g8 * 2 + kj
                            for dh in range(2):
                                pb = ps[:, kj * 1024 + dh * 512:kj * 1024 + (dh + 1) * 512]
                                vh = v_h[:, kc * 1024 + dh * 512:kc * 1024 + (dh + 1) * 512]
                                vl = v_l[:, kc * 1024 + dh * 512:kc * 1024 + (dh + 1) * 512]
                                v.tensor_copy(vh, pb)
                                v.tensor_tensor(vl, pb, vh, OP.subtract)
                        r = v.nop()
                        sV.inc(r)
                        vp_evac[g8] = sV.n

            if upto == "vphase":
                with nc.Block() as B:
                    @B.gpsimd
                    def _(g):
                        wait_all(g)
                        for kc in range(16):
                            dst = out_d[s_i * T + kc * 128:s_i * T + (kc + 1) * 128, :]
                            sG.inc(g.dma_start(out=dst, in_=v_h[:, kc * 1024:(kc + 1) * 1024]), 16)
                continue

            for qb in range(NQB):
                for mi in range(2):
                    for ac in range(DC):
                        g = mb_g[0] + mi * DC + ac
                        buf = g % 2
                        with nc.Block() as B:
                            @B.sync
                            def _(sync):
                                sync.wait_ge(sT.h, mb_sT.get(g - 2, sT.n))
                                sL.inc(sync.dma_start(
                                    out=Mb_h[buf][:], in_=m_hbm[mi][0][ac * 128:(ac + 1) * 128, :]), 16)
                                sL.inc(sync.dma_start(
                                    out=Mb_l[buf][:], in_=m_hbm[mi][1][ac * 128:(ac + 1) * 128, :]), 16)
                                mb_sL[g] = sL.n

                            @B.tensor
                            def _(t):
                                t.wait_ge(sL.h, mb_sL[g])
                                if mi == 0 and ac == 0:
                                    t.wait_ge(sV.h, sV.n)
                                ps = psA if mi == 0 else psB
                                for bc in range(DC):
                                    pb = ps[:, (bc // 2) * 512 + (bc % 2) * 256:
                                            (bc // 2) * 512 + (bc % 2) * 256 + 256]
                                    rh = hfm_h[:, ac * 2048 + qb * 256:ac * 2048 + (qb + 1) * 256]
                                    rl = hfm_l[:, ac * 2048 + qb * 256:ac * 2048 + (qb + 1) * 256]
                                    lh = Mb_h[buf][:, bc * 128:(bc + 1) * 128]
                                    ll = Mb_l[buf][:, bc * 128:(bc + 1) * 128]
                                    st0 = (ac == 0 and bc % 2 == 0)
                                    sp0 = (ac == DC - 1 and bc % 2 == 1)
                                    t.matmul(pb, lh, rh, start=st0, stop=False,
                                             skip_group_check=True)
                                    t.matmul(pb, lh, rl, start=False, stop=False,
                                             skip_group_check=True)
                                    r = t.matmul(pb, ll, rh, start=False, stop=sp0,
                                                 skip_group_check=True)
                                sT.inc(r)
                                mb_sT[g] = sT.n
                mb_g[0] += 16
                with nc.Block() as B:
                    @B.vector
                    def _(v):
                        v.wait_ge(sT.h, sT.n)
                        for mi in range(2):
                            ps = psA if mi == 0 else psB
                            for bc in range(DC):
                                pb = ps[:, (bc // 2) * 512 + (bc % 2) * 256:
                                        (bc // 2) * 512 + (bc % 2) * 256 + 256]
                                th = tmp_h[mi][:, bc * 256:(bc + 1) * 256]
                                tl = tmp_l[mi][:, bc * 256:(bc + 1) * 256]
                                v.tensor_copy(th, pb)
                                v.tensor_tensor(tl, pb, th, OP.subtract)
                        sV.inc(v.nop())

                for qt in range(2):
                    qtile = qb * 2 + qt
                    with nc.Block() as B:
                        @B.tensor
                        def _(t):
                            t.wait_ge(sV.h, sV.n if (qt == 0 or pipe["ptev"] is None)
                                      else pipe["ptev"])
                            for mi in range(2):
                                if mi == 1:
                                    t.wait_ge(sV.h, pipe["stt"] if pipe["stt"] is not None else sV.n)
                                ps = psA if mi == 0 else psB
                                for bc in range(DC):
                                    lh = tmp_h[mi][:, bc * 256 + qt * 128:bc * 256 + qt * 128 + 128]
                                    ll = tmp_l[mi][:, bc * 256 + qt * 128:bc * 256 + qt * 128 + 128]
                                    for kc in range(4):
                                        t.matmul(ps[:, kc * 512:(kc + 1) * 512], lh,
                                                 hfm_h[:, bc * 2048 + kc * 512:bc * 2048 + (kc + 1) * 512],
                                                 start=(bc == 0), stop=False)
                                    for kc in range(4):
                                        t.matmul(ps[:, kc * 512:(kc + 1) * 512], lh,
                                                 hfm_l[:, bc * 2048 + kc * 512:bc * 2048 + (kc + 1) * 512],
                                                 start=False, stop=False)
                                    for kc in range(4):
                                        r = t.matmul(ps[:, kc * 512:(kc + 1) * 512], ll,
                                                     hfm_h[:, bc * 2048 + kc * 512:bc * 2048 + (kc + 1) * 512],
                                                     start=False, stop=(bc == DC - 1))
                                sT.inc(r)
                                pipe["s1_sT" if mi == 0 else "s2_sT"] = sT.n

                        @B.vector
                        def _(v):
                            v.wait_ge(sT.h, pipe["s1_sT"])
                            sV.inc(v.reduce_max(r_nm1[:], psA[:], axis=AX.X, negate=True))
                            pipe["max1"] = sV.n

                        @B.scalar
                        def _(s):
                            s.wait_ge(sV.h, pipe["max1"])
                            sS.inc(s.activation(E1[:], psA[:], AF.Exp, bias=r_nm1[:, 0:1],
                                                scale=1.0, accum_out=r_z1[:, 0:1]))

                        @B.vector
                        def _(v):
                            v.wait_ge(sT.h, pipe["s2_sT"])
                            sV.inc(v.reduce_max(r_nm2[:], psB[:], axis=AX.X, negate=True))
                            pipe["max2"] = sV.n

                        @B.scalar
                        def _(s):
                            s.wait_ge(sV.h, pipe["max2"])
                            sS.inc(s.activation(Dm[:], psB[:], AF.Exp, bias=r_nm2[:, 0:1],
                                                scale=1.0, accum_out=r_z2[:, 0:1]))
                            pipe["exp2"] = sS.n

                        @B.vector
                        def _(v):
                            v.wait_ge(sS.h, pipe["exp2"])
                            v.reciprocal(r_t1[:], r_z2[:])
                            v.tensor_mul(r_t2[:], r_lamneg[:], r_z1[:])
                            v.tensor_mul(r_rq[:], r_z1[:], r_z1[:])
                            v.drain()
                            v.tensor_mul(r_srow[:], r_t2[:], r_t1[:])
                            v.tensor_scalar_mul(r_epsZ[:], r_rq[:], EPS)
                            v.drain()
                            sV.inc(v.scalar_tensor_tensor(Dm[:], Dm[:], r_srow[:, 0:1],
                                                          E1[:], OP.mult, OP.add))
                            pipe["dm"] = sV.n

                        ptg = {}
                        ptev = {}

                        @B.tensor
                        def _(t):
                            t.wait_ge(sV.h, pipe["dm"])
                            for gpt in range(4):
                                for j in range(4):
                                    kc = gpt * 4 + j
                                    r = t.matmul(psA[:, gpt * 512 + j * 128:gpt * 512 + (j + 1) * 128],
                                                 Dm[:, kc * 128:(kc + 1) * 128], id32[:],
                                                 is_transpose=True, start=(j == 0), stop=(j == 3))
                                sT.inc(r)
                                ptg[gpt] = sT.n

                        @B.vector
                        def _(v):
                            for gpt in range(4):
                                v.wait_ge(sT.h, ptg[gpt])
                                pb = psA[:, gpt * 512:(gpt + 1) * 512]
                                v.tensor_copy(PT_h[:, gpt * 512:(gpt + 1) * 512], pb)
                                r = v.tensor_tensor(PT_l[:, gpt * 512:(gpt + 1) * 512], pb,
                                                    PT_h[:, gpt * 512:(gpt + 1) * 512], OP.subtract)
                                sV.inc(r)
                                ptev[gpt] = sV.n

                        @B.gpsimd
                        def _(g):
                            g.wait_ge(sV.h, pipe["stt"] if pipe["stt"] is not None else sV.n)
                            sG.inc(g.dma_start(out=Rio[:],
                                               in_=rsp_d[qtile * 128:(qtile + 1) * 128, :]), 16)
                            pipe["rio"] = sG.n

                        @B.tensor
                        def _(t):
                            for gpv in range(4):
                                t.wait_ge(sV.h, ptev[gpv])
                                for j in range(4):
                                    kc = gpv * 4 + j
                                    lh = PT_h[:, kc * 128:(kc + 1) * 128]
                                    ll = PT_l[:, kc * 128:(kc + 1) * 128]
                                    first = (kc == 0)
                                    last = (kc == 15)
                                    pb0 = psB[:, 0:512]
                                    pb1 = psB[:, 512:1024]
                                    t.matmul(pb0, lh, v_h[:, kc * 1024 + 0:kc * 1024 + 512],
                                             start=first, stop=False)
                                    t.matmul(pb1, lh, v_h[:, kc * 1024 + 512:kc * 1024 + 1024],
                                             start=first, stop=False)
                                    t.matmul(pb0, lh, v_l[:, kc * 1024 + 0:kc * 1024 + 512],
                                             start=False, stop=False)
                                    t.matmul(pb1, lh, v_l[:, kc * 1024 + 512:kc * 1024 + 1024],
                                             start=False, stop=False)
                                    t.matmul(pb0, ll, v_h[:, kc * 1024 + 0:kc * 1024 + 512],
                                             start=False, stop=last)
                                    r = t.matmul(pb1, ll, v_h[:, kc * 1024 + 512:kc * 1024 + 1024],
                                                 start=False, stop=last)
                            sT.inc(r)
                            pipe["pv"] = sT.n

                        @B.vector
                        def _(v):
                            v.wait_ge(sT.h, pipe["pv"])
                            v.wait_ge(sG.h, pipe["rio"])
                            v.bn_stats(st4[:, 0:6], psB[:, 0:512])
                            v.bn_stats(st4[:, 6:12], psB[:, 512:1024])
                            v.drain()
                            v.tensor_mul(junk4[:], st4[:, 1:12:3], st4[:, 1:12:3])
                            v.reduce_sum(r_var[:], st4[:, 2:12:3], axis=AX.X)
                            v.drain()
                            v.reduce_sum(r_t1[:], junk4[:], axis=AX.X)
                            v.tensor_scalar_mul(r_t2[:], r_var[:], R1024)
                            v.drain()
                            v.scalar_tensor_tensor(r_msq[:], r_t1[:], 0.25, r_t2[:],
                                                   OP.mult, OP.add)
                            sV.inc(v.drain())
                            pipe["msq"] = sV.n

                        @B.scalar
                        def _(s):
                            s.wait_ge(sV.h, pipe["msq"])
                            s.activation(r_l[:], r_msq[:], AF.Ln, bias=r_epsZ[:, 0:1],
                                         scale=1.0)
                            sS.inc(s.activation(r_trow[:], r_l[:], AF.Exp,
                                                bias=r_ln02[:, 0:1], scale=r_mhalf[:, 0:1]))
                            pipe["trow"] = sS.n

                        @B.vector
                        def _(v):
                            v.wait_ge(sS.h, pipe["trow"])
                            sV.inc(v.scalar_tensor_tensor(Rio2[:], psB[:, 0:1024],
                                                          r_trow[:, 0:1], Rio[:],
                                                          OP.mult, OP.add))
                            pipe["stt"] = sV.n

                        @B.gpsimd
                        def _(g):
                            g.wait_ge(sV.h, pipe["stt"])
                            sG.inc(g.dma_start(out=rsp2_d[qtile * 128:(qtile + 1) * 128, :],
                                               in_=Rio2[:]), 16)
                    pipe["ptev"] = ptev[3]

            if upto == "attn":
                with nc.Block() as B:
                    @B.gpsimd
                    def _(g):
                        wait_all(g)
                        sG.inc(g.dma_start(out=out_d[s_i * T:(s_i + 1) * T, :], in_=rsp2_d[:]), 16)
                continue

            for i in range(TT):
                with nc.Block() as B:
                    @B.sync
                    def _(sync):
                        wait_all(sync, sG, sV)
                        sL.inc(sync.dma_start(out=Rio[:],
                                              in_=rsp2_d[i * 128:(i + 1) * 128, :]), 16)
                ln_tile_single(Rio[:])
                ffn_core(Rio[:], Rio2[:], r_mean[:, 0:1], r_rstd[:, 0:1])
                with nc.Block() as B:
                    @B.gpsimd
                    def _(g):
                        g.wait_ge(sV.h, sV.n)
                        dst = out_d[s_i * T + i * 128:s_i * T + (i + 1) * 128, :]
                        sG.inc(g.dma_start(out=dst, in_=Rio2[:]), 16)

        _finish(nc, ALL)
    return nc


def _finish(nc, sems):
    with nc.Block() as B:
        @B.gpsimd
        def _(g):
            for s in sems:
                if s.n:
                    g.wait_ge(s.h, s.n)


def _copyout_R(nc, R, out_d, s_i, sG, wait_all):
    with nc.Block() as B:
        @B.gpsimd
        def _(g):
            wait_all(g)
            for i in range(TT):
                dst = out_d[s_i * T + i * 128:s_i * T + (i + 1) * 128, :]
                sG.inc(g.dma_start(out=dst, in_=R[:, i * 1024:(i + 1) * 1024]), 16)


_NC_CACHE = {}


def _get_program():
    if "nc" not in _NC_CACHE:
        _NC_CACHE["nc"] = build_program()
    return _NC_CACHE["nc"]


def kernel(L0_attn_x, L0_attn_w_q12, L0_attn_w_k12, L0_attn_w_v,
           L0_attn_lambda_q1, L0_attn_lambda_k1, L0_attn_lambda_q2,
           L0_attn_lambda_k2, L0_ffn_weight, L0_ffn_bias):
    x = np.ascontiguousarray(np.asarray(L0_attn_x, dtype=np.float32))
    B, S, d = x.shape
    ncore = 8
    shards = x.reshape(ncore, (B // ncore) * S, d)

    common = {
        "wq12": np.ascontiguousarray(L0_attn_w_q12, dtype=np.float32),
        "wk12": np.ascontiguousarray(L0_attn_w_k12, dtype=np.float32),
        "wv": np.ascontiguousarray(L0_attn_w_v, dtype=np.float32),
        "fw": np.ascontiguousarray(L0_ffn_weight, dtype=np.float32),
        "fb": np.ascontiguousarray(L0_ffn_bias, dtype=np.float32),
        "lq1": np.ascontiguousarray(L0_attn_lambda_q1, dtype=np.float32),
        "lk1": np.ascontiguousarray(L0_attn_lambda_k1, dtype=np.float32),
        "lq2": np.ascontiguousarray(L0_attn_lambda_q2, dtype=np.float32),
        "lk2": np.ascontiguousarray(L0_attn_lambda_k2, dtype=np.float32),
        "id32": np.eye(128, dtype=np.float32),
        "id8": np.eye(128, dtype=np.float32).astype(ml_dtypes.float8_e4m3),
    }
    in_maps = [dict(common, x=np.ascontiguousarray(shards[c])) for c in range(ncore)]
    nc = _get_program()
    res = run_bass_kernel_spmd(nc, in_maps, core_ids=list(range(ncore)))
    out = np.concatenate([r["out"] for r in res.results], axis=0)
    return out.reshape(B, S, d)
